# version 01
# baseline (speedup 1.0000x reference)
import contextlib
import numpy as np
import concourse.bass as bass
import concourse.mybir as mybir
from concourse.bass_utils import run_bass_kernel_spmd

F32 = mybir.dt.float32
BF16 = mybir.dt.bfloat16
I32 = mybir.dt.int32
AF = mybir.ActivationFunctionType
ALU = mybir.AluOpType
AX = mybir.AxisListType.X
ds = bass.ds

NC = 8
S = 8192
D = 2048
NH = 16
DH = 128
FI = 8192
HPC = NH // NC
TPC = S // NC
NBLK = S // 512
MAGIC = float(np.float32(3 << 22))
INV_SQRT_DH = float(np.float32(1.0) / np.float32(np.sqrt(np.float32(DH))))
LN_EPS = 1e-5
RMS_EPS = 1e-6
NUMEL_W = FI * D

ENGS = ("sync", "tensor", "vector", "scalar", "gpsimd")


class Arena:
    def __init__(self, t, cols, name):
        self.t, self.cols, self.off = t, cols, 0
        self.marks, self.peak, self.name = [], 0, name

    def alloc(self, cols, rows=128):
        o = self.off
        assert o + cols <= self.cols, f"{self.name} overflow {o}+{cols}>{self.cols}"
        self.off += cols
        self.peak = max(self.peak, self.off)
        return self.t[0:rows, o:o + cols]

    def mark(self):
        self.marks.append(self.off)

    def release(self):
        self.off = self.marks.pop()


class Ctr:
    def __init__(self, sem):
        self.sem, self.n = sem, 0


def build_nc(debug=False, phases="ABCD"):
    nc = bass.Bass("TRN2", target_bir_lowering=False, debug=False)
    dt = nc.dram_tensor

    xT = dt("xT", [D, S], F32, kind="ExternalInput").ap()
    xr = dt("xr", [TPC, D], F32, kind="ExternalInput").ap()
    wq = dt("wq", [HPC, D, DH], F32, kind="ExternalInput").ap()
    wk = dt("wk", [HPC, D, DH], F32, kind="ExternalInput").ap()
    wv = dt("wv", [HPC, D, DH], F32, kind="ExternalInput").ap()
    bqkv = dt("bqkv", [3 * HPC, DH], F32, kind="ExternalInput").ap()
    wo = dt("wo", [D, D], F32, kind="ExternalInput").ap()
    bo = dt("bo", [1, D], F32, kind="ExternalInput").ap()
    wg_sl = dt("wg_sl", [D // NC, FI], F32, kind="ExternalInput").ap()
    wu_sl = dt("wu_sl", [D // NC, FI], F32, kind="ExternalInput").ap()
    wd_sl = dt("wd_sl", [FI // NC, D], F32, kind="ExternalInput").ap()
    lnw = dt("lnw", [1, FI], F32, kind="ExternalInput").ap()
    idf = dt("idf", [128, 128], F32, kind="ExternalInput").ap()

    y = dt("y", [TPC, D], F32, kind="ExternalOutput").ap()

    agz_in = dt("agz_in", [HPC * DH, S], BF16)
    agz_out = dt("agz_out", [NH * DH, S], BF16)
    ar_in = dt("ar_in", [1, 4], F32)
    ar_out = dt("ar_out", [1, 4], F32)
    agg_in = dt("agg_in", [D // NC, FI], BF16)
    agg_out = dt("agg_out", [D, FI], BF16)
    agu_in = dt("agu_in", [D // NC, FI], BF16)
    agu_out = dt("agu_out", [D, FI], BF16)
    agd_in = dt("agd_in", [FI // NC, D], BF16)
    agd_out = dt("agd_out", [FI, D], BF16)
    x2d = dt("x2d", [TPC, D], F32)
    qx2d = dt("qx2d", [D, TPC], BF16)

    dbg = {}
    if debug:
        dbg["q"] = dt("dbg_q", [128, HPC * S], F32, kind="ExternalOutput").ap()
        dbg["k"] = dt("dbg_k", [128, HPC * S], F32, kind="ExternalOutput").ap()
        dbg["v"] = dt("dbg_v", [128, HPC * S], F32, kind="ExternalOutput").ap()
        dbg["negm"] = dt("dbg_negm", [128, 32], F32, kind="ExternalOutput").ap()
        dbg["zin"] = dt("dbg_zin", [HPC * DH, S], BF16, kind="ExternalOutput").ap()
        dbg["x2"] = dt("dbg_x2", [TPC, D], F32, kind="ExternalOutput").ap()
        dbg["qx2"] = dt("dbg_qx2", [128, 8 * D], F32, kind="ExternalOutput").ap()
        dbg["terng"] = dt("dbg_terng", [128, FI], BF16, kind="ExternalOutput").ap()
        dbg["h"] = dt("dbg_h", [128, 4 * FI], BF16, kind="ExternalOutput").ap()
        dbg["qhT"] = dt("dbg_qhT", [128, 2 * FI], BF16, kind="ExternalOutput").ap()

    es = contextlib.ExitStack()
    BF_COLS = 76000
    F32_COLS = 14000
    A_BF = Arena(es.enter_context(nc.sbuf_tensor("abf", [128, BF_COLS], BF16)),
                 BF_COLS, "bf")
    A_F = Arena(es.enter_context(nc.sbuf_tensor("af", [128, F32_COLS], F32)),
                F32_COLS, "f32")
    it32 = es.enter_context(nc.sbuf_tensor("it32", [128, 1024], I32))

    psS = es.enter_context(nc.psum_tensor("psS", [128, 1024], F32))
    psS2 = es.enter_context(nc.psum_tensor("psS2", [128, 1024], F32))
    ps4 = es.enter_context(nc.psum_tensor("ps4", [128, 512], F32))
    ps5 = es.enter_context(nc.psum_tensor("ps5", [128, 512], F32))
    psRow = es.enter_context(nc.psum_tensor("psRow", [1, 1024], F32))

    def S_(name):
        return Ctr(es.enter_context(nc.semaphore(name)))

    sLD, sLD2, sGL, sST = S_("sLD"), S_("sLD2"), S_("sGL"), S_("sST")
    sTE, sAC, sVE, sCC = S_("sTE"), S_("sAC"), S_("sVE"), S_("sCC")

    plan = {e: [] for e in ENGS}

    def em(eng, fn, inc=None, k=1):
        if inc is None:
            plan[eng].append(fn)
            return None
        inc.n += k
        plan[eng].append(lambda h, f=fn, s=inc.sem, kk=k: f(h).then_inc(s, kk))
        return inc.n

    def wt(eng, ctr, val):
        if val and val > 0:
            plan[eng].append(lambda h, s=ctr.sem, v=val: h.wait_ge(s, v))

    def dml(eng, fn, inc):
        c = em(eng, fn, inc, 16)
        wt(eng, inc, c)
        return c

    def emh(eng, fn, inc):
        em(eng, fn, inc, 1)
        wt(eng, inc, inc.n)

    ones_cb = A_BF.alloc(1)
    ones_rb = A_BF.alloc(128, rows=1)
    bo_bf = A_BF.alloc(D, rows=1)
    ones_rf = A_F.alloc(128, rows=1)
    ones_cf = A_F.alloc(1)
    magic_t = A_F.alloc(1)
    nmagic_t = A_F.alloc(1)
    id_f = A_F.alloc(128)
    bqkv_sb = A_F.alloc(6)
    mw_bc = A_F.alloc(4)
    thr_bc = A_F.alloc(4)
    nthr_bc = A_F.alloc(4)
    negM = A_F.alloc(HPC * NBLK)
    absr = A_F.alloc(4)
    absp = A_F.alloc(48)
    arow = A_F.alloc(8, rows=1)
    c1c2 = A_F.alloc(16)
    c3s = A_F.alloc(8)

    em("sync", lambda h: h.dma_start(out=id_f[:, :], in_=idf[:, :]), sLD, 16)
    for j in range(6):
        em("sync", lambda h, j=j: h.dma_start(
            out=bqkv_sb[:, j:j + 1], in_=bqkv[j:j + 1, :]), sLD, 16)
    c_const = sLD.n
    em("gpsimd", lambda h: h.dma_start(out=bo_bf[:, :], in_=bo[:, :]), sGL, 16)
    for t, v in ((ones_cb, 1.0), (ones_rb, 1.0), (ones_rf, 1.0), (ones_cf, 1.0),
                 (magic_t, MAGIC), (nmagic_t, -MAGIC)):
        em("gpsimd", lambda h, t=t, v=v: h.memset(t[:, :], v))
    em("gpsimd", lambda h: h.memset(absr[:, :], 0.0), sGL, 1)
    c_gconst = sGL.n
    wt("scalar", sGL, c_gconst)
    wt("scalar", sLD, c_const)
    wt("vector", sGL, c_gconst)
    wt("tensor", sGL, c_gconst)
    wt("tensor", sLD, c_const)

    A_BF.mark(); A_F.mark()
    Qslab = A_BF.alloc(HPC * S)
    Kslab = A_BF.alloc(HPC * S)
    Vslab = A_BF.alloc(HPC * S)
    tfb = [A_F.alloc(1024) for _ in range(2)]

    def qsl(hh, c0, n):
        return Qslab[:, hh * S + c0: hh * S + c0 + n]

    def ksl(hh, c0, n):
        return Kslab[:, hh * S + c0: hh * S + c0 + n]

    def vsl(hh, c0, n):
        return Vslab[:, hh * S + c0: hh * S + c0 + n]

    ffn_chunks = []
    for wi, wsrc in enumerate((wg_sl, wu_sl, wd_sl)):
        R, Cw = (D // NC, FI) if wi < 2 else (FI // NC, D)
        for rg in range(R // 128):
            for cg in range(Cw // 1024):
                ffn_chunks.append((wi, rg, cg,
                                   wsrc[rg * 128:(rg + 1) * 128,
                                        cg * 1024:(cg + 1) * 1024]))
    NFC = len(ffn_chunks)
    last_chunk_of = {}
    for i, (wi, rg, cg, ap) in enumerate(ffn_chunks):
        last_chunk_of[wi] = i

    A_BF.mark(); A_F.mark()
    CH = 256
    NCH = S // CH
    W_sb = A_BF.alloc(6 * 16 * 128)

    def wtile(ph, c):
        return W_sb[:, (ph * 16 + c) * 128:(ph * 16 + c) * 128 + 128]

    xtb = [A_BF.alloc(16 * CH) for _ in range(2)]
    xsqb = [A_BF.alloc(CH) for _ in range(2)]
    mrow_b = A_BF.alloc(CH, rows=1)
    ncsW = A_BF.alloc(6 * 128, rows=1)
    st_m = A_F.alloc(CH, rows=1)
    st_v = A_F.alloc(CH, rows=1)
    st_s = A_F.alloc(CH, rows=1)
    st_r = A_F.alloc(CH, rows=1)
    rbc = [A_F.alloc(CH) for _ in range(2)]
    tmp_f = [A_F.alloc(CH) for _ in range(4)]

    for p, src in enumerate((wq, wk, wv)):
        for hh in range(HPC):
            ph = p * HPC + hh
            em("gpsimd", lambda h, s=src, hh=hh, ph=ph: h.dma_start(
                out=W_sb[:, ph * 2048:(ph + 1) * 2048].rearrange(
                    "p (c e) -> p c e", c=16),
                in_=s[hh].rearrange("(c p) e -> p c e", p=128)), sGL, 16)
    c_wload = sGL.n
    wt("gpsimd", sGL, c_wload)

    wt("tensor", sGL, c_wload)
    for ph in range(6):
        for c in range(16):
            em("tensor", lambda h, ph=ph, c=c: h.matmul(
                psRow[0:1, 0:128], ones_cb[:, :], wtile(ph, c),
                start=(c == 0), stop=(c == 15)), sTE, 1)
        wt("vector", sTE, sTE.n)
        em("vector", lambda h, ph=ph: h.tensor_scalar(
            ncsW[0:1, ph * 128:(ph + 1) * 128], psRow[0:1, 0:128],
            -1.0, None, op0=ALU.mult), sVE, 1)
        wt("tensor", sVE, sVE.n)

    xt_ld, qkv_done = {}, {}

    def emit_xt_load(tc):
        if tc >= NCH:
            return
        if tc >= 2:
            wt("gpsimd", sTE, qkv_done[tc - 2])
        xt_ld[tc] = dml("gpsimd", lambda h, tc=tc: h.dma_start(
            out=xtb[tc % 2][:, :].rearrange("p (c t) -> p c t", c=16),
            in_=xT[:, tc * CH:(tc + 1) * CH].rearrange("(c p) t -> p c t", p=128)),
            sGL)

    emit_xt_load(0)
    emit_xt_load(1)

    banksA = [psS[:, 0:512], psS[:, 512:1024], psS2[:, 0:512], psS2[:, 512:1024]]
    prev = dict(v_evqk=0, v_evv=0, a_vsl=0, v_alg=0, a_evqk=0)
    abs_i = 0
    abs_ld, abs_done = {}, {}
    nab = [0, 0, 0]
    t_sqsum = {}
    for tc in range(NCH):
        col = tc * CH
        xb = xtb[tc % 2]
        wt("vector", sGL, xt_ld[tc])
        wt("tensor", sGL, xt_ld[tc])
        wt("tensor", sVE, prev["v_alg"])
        for c in range(16):
            if c >= 2:
                wt("vector", sTE, t_sqsum.get((tc, c - 2), 0))
            em("vector", lambda h, c=c, xb=xb: h.tensor_tensor(
                xsqb[c % 2][:, :], xb[:, c * CH:(c + 1) * CH],
                xb[:, c * CH:(c + 1) * CH], op=ALU.mult), sVE, 1)
            v_sq = sVE.n
            em("tensor", lambda h, c=c, xb=xb: h.matmul(
                psRow[0:1, 0:CH], ones_cb[:, :], xb[:, c * CH:(c + 1) * CH],
                start=(c == 0), stop=(c == 15)), sTE, 1)
            wt("tensor", sVE, v_sq)
            em("tensor", lambda h, c=c: h.matmul(
                psRow[0:1, 512:512 + CH], ones_cb[:, :], xsqb[c % 2][:, :],
                start=(c == 0), stop=(c == 15)), sTE, 1)
            t_sqsum[(tc, c)] = sTE.n
        t_stats = sTE.n
        wt("vector", sTE, t_stats)
        emh("vector", lambda h: h.tensor_scalar(
            st_m[:, :], psRow[0:1, 0:CH], 1.0 / D, None, op0=ALU.mult), sVE)
        emh("vector", lambda h: h.tensor_tensor(
            st_v[:, :], st_m[:, :], st_m[:, :], op=ALU.mult), sVE)
        emh("vector", lambda h: h.scalar_tensor_tensor(
            st_v[:, :], psRow[0:1, 512:512 + CH], 1.0 / D, st_v[:, :],
            op0=ALU.mult, op1=ALU.subtract), sVE)
        emh("vector", lambda h: h.tensor_scalar(
            st_v[:, :], st_v[:, :], LN_EPS, None, op0=ALU.add), sVE)
        em("vector", lambda h: h.tensor_copy(mrow_b[:, :], st_m[:, :]), sVE, 1)
        v_alg = sVE.n
        prev["v_alg"] = v_alg
        wt("scalar", sVE, v_alg)
        em("scalar", lambda h: h.activation(st_s[:, :], st_v[:, :], AF.Sqrt), sAC, 1)
        a_sq = sAC.n
        wt("vector", sAC, a_sq)
        em("vector", lambda h: h.reciprocal(st_r[:, :], st_s[:, :]), sVE, 1)
        v_rr = sVE.n
        wt("tensor", sVE, v_rr)
        wt("tensor", sVE, prev["v_evv"])
        em("tensor", lambda h: h.matmul(
            ps4[:, 0:CH], ones_rf[:, :], st_r[:, :], start=True, stop=True), sTE, 1)
        wt("vector", sTE, sTE.n)
        em("vector", lambda h, tc=tc: h.tensor_copy(
            rbc[tc % 2][:, :], ps4[:, 0:CH]), sVE, 1)
        v_rbc = sVE.n
        wt("tensor", sVE, prev["v_evqk"])
        wt("tensor", sAC, prev["a_vsl"])
        for gi in range(4):
            for c in range(16):
                em("tensor", lambda h, gi=gi, c=c, xb=xb: h.matmul(
                    banksA[gi][:, 0:CH], wtile(gi, c), xb[:, c * CH:(c + 1) * CH],
                    start=(c == 0), stop=False), sTE, 1)
            em("tensor", lambda h, gi=gi: h.matmul(
                banksA[gi][:, 0:CH], ncsW[0:1, gi * 128:(gi + 1) * 128],
                mrow_b[:, :], start=False, stop=True), sTE, 1)
        t_qk = sTE.n
        wt("tensor", sVE, v_rbc)
        for gi in range(2):
            ph = 4 + gi
            bank = (ps4, ps5)[gi]
            for c in range(16):
                em("tensor", lambda h, bank=bank, ph=ph, c=c, xb=xb: h.matmul(
                    bank[:, 0:CH], wtile(ph, c), xb[:, c * CH:(c + 1) * CH],
                    start=(c == 0), stop=False), sTE, 1)
            em("tensor", lambda h, bank=bank, ph=ph: h.matmul(
                bank[:, 0:CH], ncsW[0:1, ph * 128:(ph + 1) * 128],
                mrow_b[:, :], start=False, stop=True), sTE, 1)
        t_v = sTE.n
        qkv_done[tc] = t_v
        wt("vector", sTE, t_qk)
        wt("vector", sAC, prev.get("a_evv_s", 0))
        wt("vector", sTE, prev.get("t_vt", 0))
        for gi in range(4):
            em("vector", lambda h, gi=gi, tc=tc: h.tensor_tensor(
                tmp_f[gi][:, :], banksA[gi][:, 0:CH], rbc[tc % 2][:, :],
                op=ALU.mult), sVE, 1)
        v_evqk = sVE.n
        prev["v_evqk"] = v_evqk
        wt("scalar", sVE, v_evqk)
        dsts = [qsl(0, col, CH), qsl(1, col, CH), ksl(0, col, CH), ksl(1, col, CH)]
        for gi in range(4):
            em("scalar", lambda h, gi=gi, d=dsts[gi]: h.activation(
                d, tmp_f[gi][:, :], AF.Identity,
                bias=bqkv_sb[:, gi:gi + 1], scale=1.0), sAC, 1)
        a_evqk = sAC.n
        prev["a_evqk"] = a_evqk
        wt("vector", sTE, t_v)
        wt("vector", sAC, a_evqk)
        for gi in range(2):
            bank = (ps4, ps5)[gi]
            em("vector", lambda h, bank=bank, tc=tc, gi=gi: h.tensor_tensor(
                tmp_f[gi][:, :], bank[:, 0:CH], rbc[tc % 2][:, :], op=ALU.mult), sVE, 1)
        v_evv = sVE.n
        prev["v_evv"] = v_evv
        wt("scalar", sVE, v_evv)
        for gi in range(2):
            em("scalar", lambda h, gi=gi: h.activation(
                tmp_f[2 + gi][:, :], tmp_f[gi][:, :], AF.Identity,
                bias=bqkv_sb[:, 4 + gi:5 + gi], scale=1.0), sAC, 1)
        a_evv = sAC.n
        prev["a_evv_s"] = a_evv
        wt("tensor", sVE, v_evqk)
        wt("tensor", sAC, a_evv)
        for gi in range(2):
            dstp = (psS, psS2)[gi]
            for j in range(CH // 128):
                em("tensor", lambda h, gi=gi, j=j, dstp=dstp: h.transpose(
                    dstp[:, j * 128:(j + 1) * 128],
                    tmp_f[2 + gi][:, j * 128:(j + 1) * 128], id_f[:, :]), sTE, 1)
        t_vt = sTE.n
        prev["t_vt"] = t_vt
        wt("scalar", sTE, t_vt)
        for gi in range(2):
            srcp = (psS, psS2)[gi]
            em("scalar", lambda h, gi=gi, srcp=srcp, col=col: h.activation(
                vsl(gi, col, CH), srcp[:, 0:CH], AF.Identity, scale=1.0), sAC, 1)
        a_vsl = sAC.n
        prev["a_vsl"] = a_vsl
        emit_xt_load(tc + 2)
        for _ in range(2):
            if abs_i < NFC:
                i = abs_i
                wi, rg, cg, ap = ffn_chunks[i]
                if i >= 2:
                    wt("sync", sVE, abs_done.get(i - 2, 0))
                abs_ld[i] = dml("sync", lambda h, ap=ap, i=i: h.dma_start(
                    out=tfb[i % 2][:, :], in_=ap), sLD)
                wt("vector", sLD, abs_ld[i])
                em("vector", lambda h, i=i: h.tensor_reduce(
                    absp[:, i:i + 1], tfb[i % 2][:, :], axis=AX, op=ALU.add,
                    apply_absolute_value=True), sVE, 1)
                abs_done[i] = sVE.n
                nab[wi] += 1
                abs_i += 1

    for j in range(3):
        emh("vector", lambda h, j=j: h.tensor_reduce(
            absr[:, j:j + 1], absp[:, 16 * j:16 * (j + 1)], axis=AX,
            op=ALU.add), sVE)
    wt("tensor", sVE, sVE.n)
    for j in range(3):
        em("tensor", lambda h, j=j: h.matmul(
            psRow[0:1, j:j + 1], ones_cf[:, :], absr[:, j:j + 1],
            start=True, stop=True), sTE, 1)
    t_abs = sTE.n
    wt("scalar", sTE, t_abs)
    em("scalar", lambda h: h.copy(arow[0:1, 0:4], psRow[0:1, 0:4]), sAC, 1)
    a_abs = sAC.n
    wt("gpsimd", sAC, a_abs)
    dml("gpsimd", lambda h: h.dma_start(out=ar_in[:, :], in_=arow[0:1, 0:4]), sGL)
    em("gpsimd", lambda h: h.collective_compute(
        "AllReduce", ALU.add, replica_groups=[list(range(NC))],
        ins=[ar_in.ap().opt()], outs=[ar_out.ap().opt()]), sCC, 1)
    cc_ar = sCC.n

    A_BF.release(); A_F.release()

    A_BF.mark(); A_F.mark()
    Pbuf = [A_BF.alloc(1024) for _ in range(2)]
    PMb = [A_BF.alloc(1024) for _ in range(2)]
    msk_sb = A_BF.alloc(2048)
    sqp = [A_BF.alloc(512) for _ in range(2)]
    ztile = [A_BF.alloc(512) for _ in range(2)]
    tob = [A_BF.alloc(1024) for _ in range(2)]
    t1b = A_BF.alloc(1024)
    t2b = A_BF.alloc(1024)
    mq2 = A_F.alloc(HPC * NBLK, rows=1)
    mk2 = A_F.alloc(HPC * NBLK, rows=1)
    mrow = A_F.alloc(HPC * NBLK, rows=1)
    lrow = A_F.alloc(512, rows=1)
    rlrow = A_F.alloc(512, rows=1)
    rlbc = A_F.alloc(512)

    em("gpsimd", lambda h: h.iota(
        it32[:, :], [[-128, 2], [1, 512]], base=0, channel_multiplier=-1), sGL, 1)
    g_io1 = sGL.n
    wt("vector", sGL, g_io1)
    em("vector", lambda h: h.tensor_scalar(
        msk_sb[:, 0:1024], it32[:, :], 0.0, None, op0=ALU.is_ge), sVE, 1)
    v_m1 = sVE.n
    wt("gpsimd", sVE, v_m1)
    em("gpsimd", lambda h: h.iota(
        it32[:, :], [[-128, 2], [1, 512]], base=-256, channel_multiplier=-1), sGL, 1)
    g_io2 = sGL.n
    wt("vector", sGL, g_io2)
    em("vector", lambda h: h.tensor_scalar(
        msk_sb[:, 1024:2048], it32[:, :], 0.0, None, op0=ALU.is_ge), sVE, 1)

    wt("vector", sAC, prev["a_vsl"])
    wt("tensor", sAC, a_abs)
    slot_t, slot_v = {}, {}
    slot = 0
    for hh in range(HPC):
        for qk in range(2):
            slab = (qsl, ksl)[qk]
            dstrow = (mq2, mk2)[qk]
            for g in range(NBLK):
                if slot >= 2:
                    wt("vector", sTE, slot_t.get(slot - 2, 0))
                em("vector", lambda h, hh=hh, g=g, slab=slab, slot=slot: h.tensor_tensor(
                    sqp[slot % 2][:, :], slab(hh, g * 512, 512),
                    slab(hh, g * 512, 512), op=ALU.mult), sVE, 1)
                v_s = sVE.n
                wt("tensor", sVE, v_s)
                if slot >= 2:
                    wt("tensor", sVE, slot_v.get(slot - 2, 0))
                em("tensor", lambda h, slot=slot: h.matmul(
                    psRow[0:1, (slot % 2) * 512:(slot % 2) * 512 + 512],
                    ones_cb[:, :], sqp[slot % 2][:, :], start=True, stop=True), sTE, 1)
                slot_t[slot] = sTE.n
                wt("vector", sTE, slot_t[slot])
                emh("vector", lambda h, hh=hh, g=g, dstrow=dstrow, slot=slot: h.tensor_reduce(
                    dstrow[0:1, hh * NBLK + g:hh * NBLK + g + 1],
                    psRow[0:1, (slot % 2) * 512:(slot % 2) * 512 + 512],
                    axis=AX, op=ALU.max), sVE)
                slot_v[slot] = sVE.n
                slot += 1
        for g in range(1, NBLK):
            emh("vector", lambda h, hh=hh, g=g: h.tensor_tensor(
                mk2[0:1, hh * NBLK + g:hh * NBLK + g + 1],
                mk2[0:1, hh * NBLK + g - 1:hh * NBLK + g],
                mk2[0:1, hh * NBLK + g:hh * NBLK + g + 1], op=ALU.max), sVE)
    emh("vector", lambda h: h.tensor_tensor(
        mrow[:, :], mq2[:, :], mk2[:, :], op=ALU.mult), sVE)
    v_mm = sVE.n
    wt("scalar", sVE, v_mm)
    em("scalar", lambda h: h.activation(mrow[:, :], mrow[:, :], AF.Sqrt), sAC, 1)
    a_ms = sAC.n
    wt("vector", sAC, a_ms)
    emh("vector", lambda h: h.tensor_scalar(
        mrow[:, :], mrow[:, :], -INV_SQRT_DH, None, op0=ALU.mult), sVE)
    v_mn = sVE.n
    wt("tensor", sVE, v_mn)
    em("tensor", lambda h: h.matmul(
        ps4[:, 0:HPC * NBLK], ones_rf[:, :], mrow[:, :], start=True, stop=True), sTE, 1)
    wt("vector", sTE, sTE.n)
    em("vector", lambda h: h.tensor_copy(negM[:, :], ps4[:, 0:HPC * NBLK]), sVE, 1)
    v_negm = sVE.n
    wt("scalar", sVE, v_negm)

    wt("sync", sCC, cc_ar)
    ld_ar = dml("sync", lambda h: h.dma_start(
        out=arow[0:1, 4:8], in_=ar_out[:, :]), sLD)
    wt("vector", sLD, ld_ar)
    emh("vector", lambda h: h.tensor_scalar(
        arow[0:1, 4:8], arow[0:1, 4:8], 1.0 / NUMEL_W, 1e-5,
        op0=ALU.mult, op1=ALU.max), sVE)
    v_mw = sVE.n
    wt("tensor", sVE, v_mw)
    em("tensor", lambda h: h.matmul(
        ps5[:, 0:4], ones_rf[:, :], arow[0:1, 4:8], start=True, stop=True), sTE, 1)
    t_mwb = sTE.n
    wt("scalar", sTE, t_mwb)
    em("scalar", lambda h: h.activation(
        mw_bc[:, :], ps5[:, 0:4], AF.Identity, scale=1.0), sAC, 1)
    em("scalar", lambda h: h.activation(
        thr_bc[:, :], ps5[:, 0:4], AF.Identity, scale=0.5), sAC, 1)
    em("scalar", lambda h: h.activation(
        nthr_bc[:, :], ps5[:, 0:4], AF.Identity, scale=-0.5), sAC, 1)
    a_thr = sAC.n
    wt("vector", sAC, a_thr)

    tern_state = dict(i=0)
    tern_ld, tern_done, tern_pub = {}, {}, {}
    ag_in_t = (agg_in, agu_in, agd_in)
    ag_out_t = (agg_out, agu_out, agd_out)

    def emit_tern(nmax):
        for _ in range(nmax):
            i = tern_state["i"]
            if i >= NFC:
                return
            wi, rg, cg, ap = ffn_chunks[i]
            if i >= 2:
                wt("sync", sVE, tern_done.get(i - 2, 0))
            else:
                wt("sync", sVE, abs_done[NFC - 2 + i])
            tern_ld[i] = dml("sync", lambda h, ap=ap, i=i: h.dma_start(
                out=tfb[i % 2][:, :], in_=ap), sLD)
            wt("vector", sLD, tern_ld[i])
            if i >= 2:
                wt("vector", sGL, tern_pub.get(i - 2, 0))
            em("vector", lambda h, i=i, wi=wi: h.tensor_scalar(
                t1b[:, :], tfb[i % 2][:, :], thr_bc[:, wi:wi + 1], None,
                op0=ALU.is_ge), sVE, 1)
            em("vector", lambda h, i=i, wi=wi: h.tensor_scalar(
                t2b[:, :], tfb[i % 2][:, :], nthr_bc[:, wi:wi + 1], None,
                op0=ALU.is_le), sVE, 1)
            em("vector", lambda h, i=i: h.tensor_tensor(
                tob[i % 2][:, :], t1b[:, :], t2b[:, :], op=ALU.subtract), sVE, 1)
            tern_done[i] = sVE.n
            wt("gpsimd", sVE, tern_done[i])
            dst = ag_in_t[wi]
            tern_pub[i] = dml("gpsimd", lambda h, dst=dst, rg=rg, cg=cg, i=i: h.dma_start(
                out=dst[rg * 128:(rg + 1) * 128, cg * 1024:(cg + 1) * 1024],
                in_=tob[i % 2][:, :]), sGL)
            if i == last_chunk_of[wi]:
                wt("gpsimd", sGL, tern_pub[i])
                em("gpsimd", lambda h, wi=wi: h.collective_compute(
                    "AllGather", ALU.bypass, replica_groups=[list(range(NC))],
                    ins=[ag_in_t[wi].ap().opt()],
                    outs=[ag_out_t[wi].ap().opt()]), sCC, 1)
            tern_state["i"] = i + 1

    zpub = {}
    prevB = dict(z_ev=v_negm, l_rd=v_negm)
    last_exp_bank = [0, 0]
    last_av_pbuf = [0, 0]
    last_msk_pm = [0, 0]
    for hh in range(HPC):
        for g in range(NBLK):
            idx = hh * NBLK + g
            nkt = 4 * (g + 1)
            npair = nkt // 2
            qmv = qsl(hh, g * 512, 512)
            for p in range(npair):
                kt0 = 2 * p
                pb = psS if p % 2 == 0 else psS2
                wt("tensor", sAC, last_exp_bank[p % 2])
                for j in range(2):
                    kt = kt0 + j
                    em("tensor", lambda h, hh=hh, kt=kt, j=j, pb=pb, qmv=qmv: h.matmul(
                        pb[:, j * 512:(j + 1) * 512],
                        ksl(hh, kt * 128, 128), qmv, start=True, stop=True), sTE, 1)
                t_sc = sTE.n
                wt("scalar", sTE, t_sc)
                wt("scalar", sTE, last_av_pbuf[p % 2])
                em("scalar", lambda h, pb=pb, p=p, idx=idx: h.activation(
                    Pbuf[p % 2][:, :], pb[:, :], AF.Exp,
                    bias=negM[:, idx:idx + 1], scale=INV_SQRT_DH), sAC, 1)
                e_d = sAC.n
                last_exp_bank[p % 2] = e_d
                diag = (p >= npair - 2)
                src = Pbuf[p % 2]
                if diag:
                    wt("vector", sAC, e_d)
                    wt("vector", sTE, last_msk_pm[p % 2])
                    moff = 0 if p == npair - 2 else 1024
                    em("vector", lambda h, p=p, moff=moff: h.tensor_tensor(
                        PMb[p % 2][:, :], Pbuf[p % 2][:, :],
                        msk_sb[:, moff:moff + 1024], op=ALU.mult), sVE, 1)
                    wt("tensor", sVE, sVE.n)
                    src = PMb[p % 2]
                else:
                    wt("tensor", sAC, e_d)
                if p == 0:
                    wt("tensor", sVE, prevB["l_rd"])
                    wt("tensor", sVE, prevB["z_ev"])
                for j in range(2):
                    kt = kt0 + j
                    em("tensor", lambda h, src=src, j=j, kt=kt, nkt=nkt: h.matmul(
                        psRow[0:1, 0:512], ones_cb[:, :],
                        src[:, j * 512:(j + 1) * 512],
                        start=(kt == 0), stop=(kt == nkt - 1)), sTE, 1)
                    em("tensor", lambda h, src=src, j=j, kt=kt, hh=hh: h.matmul(
                        ps4[:, :], vsl(hh, kt * 128, 128),
                        src[:, j * 512:(j + 1) * 512],
                        start=(kt == 0), stop=(kt == nkt - 1)), sTE, 1)
                last_av_pbuf[p % 2] = sTE.n
                if diag:
                    last_msk_pm[p % 2] = sTE.n
            t_av = sTE.n
            wt("vector", sTE, t_av)
            emh("vector", lambda h: h.tensor_scalar(
                lrow[:, :], psRow[0:1, 0:512], 1e-30, None, op0=ALU.max), sVE)
            em("vector", lambda h: h.reciprocal(rlrow[:, :], lrow[:, :]), sVE, 1)
            prevB["l_rd"] = sVE.n
            wt("tensor", sVE, sVE.n)
            em("tensor", lambda h: h.matmul(
                ps5[:, :], ones_rf[:, :], rlrow[:, :], start=True, stop=True), sTE, 1)
            wt("vector", sTE, sTE.n)
            em("vector", lambda h: h.tensor_copy(rlbc[:, :], ps5[:, :]), sVE, 1)
            if idx >= 2:
                wt("vector", sGL, zpub.get(idx - 2, 0))
            em("vector", lambda h, idx=idx: h.tensor_tensor(
                ztile[idx % 2][:, :], ps4[:, :], rlbc[:, :], op=ALU.mult), sVE, 1)
            prevB["z_ev"] = sVE.n
            wt("gpsimd", sVE, sVE.n)
            zpub[idx] = dml("gpsimd", lambda h, hh=hh, g=g, idx=idx: h.dma_start(
                out=agz_in[hh * 128:(hh + 1) * 128, g * 512:(g + 1) * 512],
                in_=ztile[idx % 2][:, :]), sGL)
            emit_tern(2)

    emit_tern(NFC)
    wt("gpsimd", sGL, zpub[HPC * NBLK - 1])
    em("gpsimd", lambda h: h.collective_compute(
        "AllGather", ALU.bypass, replica_groups=[list(range(NC))],
        ins=[agz_in.ap().opt()], outs=[agz_out.ap().opt()]), sCC, 1)
    cc_z = sCC.n

    if debug:
        wt("gpsimd", sAC, sAC.n)
        wt("gpsimd", sVE, sVE.n)
        wt("gpsimd", sTE, sTE.n)
        wt("gpsimd", sCC, cc_z)
        for name, slab in (("q", Qslab), ("k", Kslab), ("v", Vslab)):
            em("gpsimd", lambda h, n=name, s=slab: h.dma_start(
                out=dbg[n][:, :], in_=s[:, :]), sST, 16)
        em("gpsimd", lambda h: h.dma_start(
            out=dbg["negm"][:, :], in_=negM[:, :]), sST, 16)
        em("gpsimd", lambda h: h.dma_start(
            out=dbg["zin"][:, :], in_=agz_in.ap()), sST, 16)
        em("gpsimd", lambda h: h.dma_start(
            out=dbg["terng"][:, :], in_=agg_out[0:128, :]), sST, 16)

    A_BF.release(); A_F.release()
    A_BF.release(); A_F.release()
    vB_end = sVE.n
    gB_end = sGL.n

    if "C" not in phases:
        wt("gpsimd", sST, sST.n)
        _finish(nc, plan)
        es.close()
        return nc

    A_BF.mark(); A_F.mark()

    A_BF.mark(); A_F.mark()
    Zfull = A_BF.alloc(16 * 1024)
    WOsb = A_BF.alloc(16 * 2048)
    qxe = [A_BF.alloc(512) for _ in range(2)]
    xbuf = [A_F.alloc(4 * 512) for _ in range(2)]
    x2t = [A_F.alloc(512) for _ in range(2)]
    x2row = [A_F.alloc(D) for _ in range(2)]
    scrA = A_F.alloc(D)
    scrB = A_F.alloc(D)
    st2 = A_F.alloc(8)
    sA2 = A_F.alloc(1)
    sB2 = A_F.alloc(1)
    sden = A_F.alloc(1)
    smx = A_F.alloc(1)
    smn = A_F.alloc(1)

    pidc = {}

    def _pid(h):
        pidc["tok0"] = h.compute_val(h.partition_id() * TPC)
    plan["sync"].append(_pid)

    wt("tensor", sAC, sAC.n)
    wt("sync", sCC, cc_z)
    for ch in range(16):
        em("sync", lambda h, ch=ch: h.dma_start(
            out=Zfull[:, ch * 1024:(ch + 1) * 1024],
            in_=agz_out[ch * 128:(ch + 1) * 128, ds(pidc["tok0"], TPC)]), sLD, 16)
    c_zf = sLD.n
    wt("sync", sLD, c_zf)

    banksC = [psS[:, 0:512], psS[:, 512:1024], psS2[:, 0:512], psS2[:, 512:1024]]
    c_wo = dml("gpsimd", lambda h: h.dma_start(
        out=WOsb[:, :].rearrange("p (c d) -> p c d", c=16),
        in_=wo.rearrange("(c p) d -> p c d", p=128)), sGL)
    x2st = {}
    prevC = dict(ev=vB_end)
    li = 0
    for db in range(4):
        for grp in range(2):
            tts = [grp * 4 + i for i in range(4)]
            for ch in range(16):
                if (db, grp, ch) == (0, 0, 0):
                    wt("tensor", sGL, c_wo)
                if ch == 0:
                    wt("tensor", sLD, c_zf)
                    wt("tensor", sVE, prevC["ev"])
                for i, tt in enumerate(tts):
                    em("tensor", lambda h, ch=ch, db=db, tt=tt, i=i: h.matmul(
                        banksC[i],
                        Zfull[:, ch * 1024 + tt * 128:ch * 1024 + tt * 128 + 128],
                        WOsb[:, ch * 2048 + db * 512:ch * 2048 + db * 512 + 512],
                        start=(ch == 0), stop=False), sTE, 1)
            for i, tt in enumerate(tts):
                em("tensor", lambda h, i=i, db=db: h.matmul(
                    banksC[i], ones_rb[:, :], bo_bf[0:1, db * 512:(db + 1) * 512],
                    start=False, stop=True), sTE, 1)
            t_op = sTE.n
            gi = db * 2 + grp
            qe = "sync"
            qc = sLD
            if gi >= 2:
                wt(qe, sVE, grp_ev.get(gi - 2, 0))
            xl = dml(qe, lambda h, db=db, grp=grp, gi=gi: h.dma_start(
                out=xbuf[gi % 2][:, :].rearrange("p (i d) -> p i d", i=4),
                in_=xr[grp * 512:(grp + 1) * 512,
                       db * 512:(db + 1) * 512].rearrange(
                    "(i p) d -> p i d", p=128)), qc)
            for i, tt in enumerate(tts):
                wt("vector", sTE, t_op)
                wt("vector", qc, xl)
                if li >= 2:
                    wt("vector", sGL, x2st.get(li - 2, 0))
                em("vector", lambda h, i=i, li=li, gi=gi: h.tensor_tensor(
                    x2t[li % 2][:, :], banksC[i],
                    xbuf[gi % 2][:, i * 512:(i + 1) * 512], op=ALU.add), sVE, 1)
                if li < 2:
                    x2ev = x2ev if li else {}
                x2ev[li] = sVE.n
                prevC["ev"] = sVE.n
                wt("gpsimd", sVE, sVE.n)
                x2st[li] = dml("gpsimd", lambda h, tt=tt, db=db, li=li: h.dma_start(
                    out=x2d[tt * 128:(tt + 1) * 128, db * 512:(db + 1) * 512],
                    in_=x2t[li % 2][:, :]), sGL)
                li += 1
            if gi < 2:
                grp_ev = grp_ev if gi else {}
            grp_ev[gi] = sVE.n

    if debug:
        wt("gpsimd", sGL, sGL.n)
        em("gpsimd", lambda h: h.dma_start(
            out=dbg["x2"][:, :], in_=x2d.ap()), sST, 16)

    rnd_done = {}
    tT_prev = 0
    prevT = dict(ev=0)
    for tt in range(8):
        need = max(x2st[db * 8 + tt] for db in range(4))
        wt("sync", sGL, need)
        if tt >= 2:
            wt("sync", sVE, rnd_done.get(tt - 2, 0))
        x2l = dml("sync", lambda h, tt=tt: h.dma_start(
            out=x2row[tt % 2][:, :], in_=x2d[tt * 128:(tt + 1) * 128, :]), sLD)
        wt("scalar", sLD, x2l)
        if tt >= 1:
            wt("scalar", sTE, tT_prev)
        em("scalar", lambda h, tt=tt: h.activation(
            scrA[:, :], x2row[tt % 2][:, :], AF.Square,
            accum_out=st2[:, 0:1]), sAC, 1)
        a_ss = sAC.n
        wt("vector", sLD, x2l)
        wt("vector", sTE, tT_prev)
        emh("vector", lambda h, tt=tt: h.tensor_reduce(
            st2[:, 1:2], x2row[tt % 2][:, :], axis=AX, op=ALU.add), sVE)
        emh("vector", lambda h: h.tensor_scalar(
            st2[:, 2:3], st2[:, 1:2], 1.0 / D, None, op0=ALU.mult), sVE)
        wt("vector", sAC, a_ss)
        emh("vector", lambda h: h.tensor_tensor(
            st2[:, 3:4], st2[:, 2:3], st2[:, 2:3], op=ALU.mult), sVE)
        emh("vector", lambda h: h.scalar_tensor_tensor(
            st2[:, 3:4], st2[:, 0:1], 1.0 / D, st2[:, 3:4],
            op0=ALU.mult, op1=ALU.subtract), sVE)
        emh("vector", lambda h: h.tensor_scalar(
            st2[:, 3:4], st2[:, 3:4], LN_EPS, None, op0=ALU.add), sVE)
        v_var = sVE.n
        wt("scalar", sVE, v_var)
        em("scalar", lambda h: h.activation(st2[:, 4:5], st2[:, 3:4], AF.Sqrt), sAC, 1)
        a_sd = sAC.n
        wt("vector", sAC, a_sd)
        emh("vector", lambda h: h.reciprocal(st2[:, 5:6], st2[:, 4:5]), sVE)
        emh("vector", lambda h, tt=tt: h.tensor_scalar(
            scrA[:, :], x2row[tt % 2][:, :], st2[:, 2:3], -1e30,
            op0=ALU.subtract, op1=ALU.max, accum_out=smx[:, :]), sVE)
        emh("vector", lambda h, tt=tt: h.tensor_scalar(
            scrA[:, :], x2row[tt % 2][:, :], st2[:, 2:3], 1e30,
            op0=ALU.subtract, op1=ALU.min, accum_out=smn[:, :]), sVE)
        emh("vector", lambda h: h.tensor_scalar(
            smn[:, :], smn[:, :], -1.0, None, op0=ALU.mult), sVE)
        emh("vector", lambda h: h.tensor_tensor(
            smx[:, :], smx[:, :], smn[:, :], op=ALU.max), sVE)
        emh("vector", lambda h: h.tensor_tensor(
            sden[:, :], smx[:, :], st2[:, 5:6], op=ALU.mult), sVE)
        emh("vector", lambda h: h.tensor_scalar(
            sden[:, :], sden[:, :], 1e-5, None, op0=ALU.max), sVE)
        emh("vector", lambda h: h.reciprocal(sA2[:, :], sden[:, :]), sVE)
        emh("vector", lambda h: h.tensor_scalar(
            sA2[:, :], sA2[:, :], 127.0, None, op0=ALU.mult), sVE)
        emh("vector", lambda h: h.tensor_tensor(
            sA2[:, :], sA2[:, :], st2[:, 5:6], op=ALU.mult), sVE)
        emh("vector", lambda h: h.tensor_tensor(
            sB2[:, :], st2[:, 2:3], sA2[:, :], op=ALU.mult), sVE)
        emh("vector", lambda h: h.tensor_scalar(
            sB2[:, :], sB2[:, :], -1.0, None, op0=ALU.mult), sVE)
        emh("vector", lambda h, tt=tt: h.scalar_tensor_tensor(
            c1c2[:, 2 * tt:2 * tt + 1], sden[:, :], 1.0 / 127.0, mw_bc[:, 0:1],
            op0=ALU.mult, op1=ALU.mult), sVE)
        emh("vector", lambda h, tt=tt: h.scalar_tensor_tensor(
            c1c2[:, 2 * tt + 1:2 * tt + 2], sden[:, :], 1.0 / 127.0, mw_bc[:, 1:2],
            op0=ALU.mult, op1=ALU.mult), sVE)
        v_ab = sVE.n
        em("vector", lambda h, tt=tt: h.tensor_scalar(
            scrA[:, :], x2row[tt % 2][:, :], sA2[:, :], sB2[:, :],
            op0=ALU.mult, op1=ALU.add), sVE, 1)
        em("vector", lambda h: h.tensor_scalar(
            scrB[:, :], scrA[:, :], MAGIC, -MAGIC,
            op0=ALU.add, op1=ALU.add), sVE, 1)
        v_rnd = sVE.n
        rnd_done[tt] = v_rnd
        wt("tensor", sVE, v_rnd)
        for cq in range(4):
            qi = (tt * 4 + cq)
            wt("tensor", sAC, prevT["ev"])
            for j in range(4):
                em("tensor", lambda h, cq=cq, j=j: h.transpose(
                    ps5[:, j * 128:(j + 1) * 128],
                    scrB[:, (cq * 4 + j) * 128:(cq * 4 + j) * 128 + 128],
                    id_f[:, :]), sTE, 1)
            t_t = sTE.n
            tT_prev = t_t
            wt("scalar", sTE, t_t)
            if qi >= 2:
                wt("scalar", sGL, qx_st.get(qi - 2, 0))
            em("scalar", lambda h, qi=qi: h.activation(
                qxe[qi % 2][:, :], ps5[:, :], AF.Identity, scale=1.0), sAC, 1)
            prevT["ev"] = sAC.n
            wt("gpsimd", sAC, sAC.n)
            if qi < 2:
                qx_st = qx_st if qi else {}
            qx_st[qi] = dml("gpsimd", lambda h, tt=tt, cq=cq, qi=qi: h.dma_start(
                out=qx2d[cq * 512:(cq + 1) * 512,
                         tt * 128:(tt + 1) * 128].rearrange(
                    "(j p) t -> p j t", p=128),
                in_=qxe[qi % 2][:, :].rearrange("p (j t) -> p j t", j=4)), sGL)

    if debug:
        wt("gpsimd", sGL, sGL.n)
        em("gpsimd", lambda h: h.dma_start(
            out=dbg["qx2"][:, :].rearrange("p (c t) -> p c t", c=16),
            in_=qx2d.ap().rearrange("(c p) t -> p c t", p=128)), sST, 16)
    A_BF.release(); A_F.release()
    vC_end = sVE.n
    aC_end = sAC.n
    gC_end = sGL.n

    if "D" not in phases:
        wt("gpsimd", sST, sST.n)
        _finish(nc, plan)
        es.close()
        return nc

    A_BF.mark(); A_F.mark()
    hslab = A_BF.alloc(4 * FI)
    qhT = A_BF.alloc(2 * FI)
    wsl = A_BF.alloc(16 * 512)
    qxTh = A_BF.alloc(4 * D)
    g2t = [A_BF.alloc(512) for _ in range(4)]
    scr_a = A_BF.alloc(512)
    hsq = A_BF.alloc(512)
    lnwf = [A_F.alloc(512, rows=1) for _ in range(2)]
    wbcv = [A_F.alloc(512) for _ in range(2)]
    scf = [A_F.alloc(512) for _ in range(2)]
    x2b = [A_F.alloc(2 * 512) for _ in range(2)]
    yb = [A_F.alloc(512) for _ in range(2)]
    ssq_p = A_F.alloc(64)
    wmx_p = A_F.alloc(64)
    sdh = A_F.alloc(1)
    srr = A_F.alloc(1)
    sA3 = A_F.alloc(4)

    wt("scalar", sVE, vC_end)
    wt("sync", sGL, gC_end)
    wt("tensor", sAC, aC_end)

    banksD = [psS[:, 0:512], psS[:, 512:1024], psS2[:, 0:512], psS2[:, 512:1024]]
    y_st = {}
    yli = 0
    for half in range(2):
        tts = [half * 4 + i for i in range(4)]
        for til in range(4):
            em("sync", lambda h, half=half, til=til: h.dma_start(
                out=qxTh[:, til * 2048:(til + 1) * 2048].rearrange(
                    "p (c k) -> p c k", c=16),
                in_=qx2d[:, half * 512 + til * 128:half * 512 + (til + 1) * 128
                         ].rearrange("(c p) k -> p c k", p=128)), sLD, 16)
        qxl = sLD.n
        wt("sync", sLD, qxl)

        v_comb_prev = sVE.n
        a_g2 = sAC.n
        wbc_rd_prev = sVE.n
        lnmm = {}
        for ib in range(16):
            par = ib % 2
            if ib >= 2:
                wt("sync", sTE, lnmm[ib - 2])
            lnl = dml("sync", lambda h, ib=ib, par=par: h.dma_start(
                out=lnwf[par][:, :], in_=lnw[0:1, ib * 512:(ib + 1) * 512]), sLD)
            wt("tensor", sLD, lnl)
            if ib >= 1:
                wt("tensor", sAC, wbc_ev)
            em("tensor", lambda h, par=par: h.matmul(
                ps5[:, :], ones_rf[:, :], lnwf[par][:, :],
                start=True, stop=True), sTE, 1)
            t_wb = sTE.n
            lnmm[ib] = t_wb
            wt("scalar", sTE, t_wb)
            if ib >= 2:
                wt("scalar", sVE, wbc_rd.get(ib - 2, 0))
            em("scalar", lambda h, par=par: h.activation(
                wbcv[par][:, :], ps5[:, :], AF.Identity, scale=1.0), sAC, 1)
            wbc_ev = sAC.n
            if half or ib:
                wt("sync", sTE, t_wsl_free)
            gl = dml("sync", lambda h, ib=ib: h.dma_start(
                out=wsl[:, :].rearrange("p (c i) -> p c i", c=16),
                in_=agg_out[:, ib * 512:(ib + 1) * 512].rearrange(
                    "(c p) i -> p c i", p=128)), sLD)
            wt("tensor", sLD, gl)
            if ib == 0:
                wt("tensor", sLD, qxl)
            wt("tensor", sVE, v_comb_prev)
            for ch in range(16):
                for i in range(4):
                    em("tensor", lambda h, ch=ch, i=i: h.matmul(
                        banksD[i],
                        qxTh[:, i * 2048 + ch * 128:i * 2048 + ch * 128 + 128],
                        wsl[:, ch * 512:(ch + 1) * 512],
                        start=(ch == 0), stop=(ch == 15)), sTE, 1)
            t_g = sTE.n
            wt("scalar", sTE, t_g)
            for i in range(4):
                em("scalar", lambda h, i=i, tti=tts[i]: h.activation(
                    g2t[i][:, :], banksD[i], AF.Silu,
                    scale=c1c2[:, 2 * tti:2 * tti + 1]), sAC, 1)
            a_g2 = sAC.n
            wt("sync", sTE, t_g)
            ul = dml("sync", lambda h, ib=ib: h.dma_start(
                out=wsl[:, :].rearrange("p (c i) -> p c i", c=16),
                in_=agu_out[:, ib * 512:(ib + 1) * 512].rearrange(
                    "(c p) i -> p c i", p=128)), sLD)
            wt("tensor", sLD, ul)
            wt("tensor", sAC, a_g2)
            for ch in range(16):
                for i in range(4):
                    em("tensor", lambda h, ch=ch, i=i: h.matmul(
                        banksD[i],
                        qxTh[:, i * 2048 + ch * 128:i * 2048 + ch * 128 + 128],
                        wsl[:, ch * 512:(ch + 1) * 512],
                        start=(ch == 0), stop=(ch == 15)), sTE, 1)
            t_u = sTE.n
            t_wsl_free = t_u
            wt("vector", sTE, t_u)
            wt("vector", sAC, a_g2)
            for i in range(4):
                hdst = hslab[:, i * FI + ib * 512:i * FI + ib * 512 + 512]
                em("vector", lambda h, i=i, tti=tts[i], hdst=hdst: h.scalar_tensor_tensor(
                    hdst, banksD[i], c1c2[:, 2 * tti + 1:2 * tti + 2],
                    g2t[i][:, :], op0=ALU.mult, op1=ALU.mult), sVE, 1)
                em("vector", lambda h, hdst=hdst: h.tensor_tensor(
                    hsq[:, :], hdst, hdst, op=ALU.mult), sVE, 1)
                em("vector", lambda h, i=i, ib=ib: h.tensor_reduce(
                    ssq_p[:, i * 16 + ib:i * 16 + ib + 1], hsq[:, :],
                    axis=AX, op=ALU.add), sVE, 1)
                em("vector", lambda h, hdst=hdst, par=par: h.tensor_tensor(
                    scr_a[:, :], hdst, wbcv[par][:, :], op=ALU.mult), sVE, 1)
                em("vector", lambda h, i=i, ib=ib: h.tensor_reduce(
                    wmx_p[:, i * 16 + ib:i * 16 + ib + 1], scr_a[:, :],
                    axis=AX, op=ALU.max, apply_absolute_value=True), sVE, 1)
            v_comb_prev = sVE.n
            if ib < 2:
                wbc_rd = wbc_rd if ib else {}
            wbc_rd[ib] = sVE.n

        if debug and half == 0:
            wt("gpsimd", sVE, sVE.n)
            em("gpsimd", lambda h: h.dma_start(
                out=dbg["h"][:, :], in_=hslab[:, :]), sST, 16)
        for i in range(4):
            tti = tts[i]
            emh("vector", lambda h, i=i: h.tensor_reduce(
                sdh[:, :], ssq_p[:, i * 16:(i + 1) * 16],
                axis=AX, op=ALU.add), sVE)
            emh("vector", lambda h: h.tensor_scalar(
                sdh[:, :], sdh[:, :], 1.0 / FI, RMS_EPS,
                op0=ALU.mult, op1=ALU.add), sVE)
            wt("scalar", sVE, sVE.n)
            em("scalar", lambda h: h.activation(sdh[:, :], sdh[:, :], AF.Sqrt), sAC, 1)
            wt("vector", sAC, sAC.n)
            emh("vector", lambda h: h.reciprocal(srr[:, :], sdh[:, :]), sVE)
            emh("vector", lambda h, i=i: h.tensor_reduce(
                sdh[:, :], wmx_p[:, i * 16:(i + 1) * 16],
                axis=AX, op=ALU.max), sVE)
            emh("vector", lambda h: h.tensor_tensor(
                sdh[:, :], srr[:, :], sdh[:, :], op=ALU.mult), sVE)
            emh("vector", lambda h: h.tensor_scalar(
                sdh[:, :], sdh[:, :], 1e-5, None, op0=ALU.max), sVE)
            emh("vector", lambda h, tti=tti: h.scalar_tensor_tensor(
                c3s[:, tti:tti + 1], sdh[:, :], 1.0 / 127.0, mw_bc[:, 2:3],
                op0=ALU.mult, op1=ALU.mult), sVE)
            emh("vector", lambda h: h.reciprocal(sdh[:, :], sdh[:, :]), sVE)
            emh("vector", lambda h: h.tensor_scalar(
                sdh[:, :], sdh[:, :], 127.0, None, op0=ALU.mult), sVE)
            emh("vector", lambda h, i=i: h.tensor_tensor(
                sA3[:, i:i + 1], srr[:, :], sdh[:, :], op=ALU.mult), sVE)
        v_d2 = sVE.n

        for pr in range(2):
            iis = [2 * pr, 2 * pr + 1]
            prevQ = dict(ev=0)
            wt("vector", sVE, v_d2)
            wt("scalar", sTE, sTE.n)
            rnd_prev = 0
            lnmm3 = {}
            for icb in range(16):
                par = icb % 2
                if icb >= 2:
                    wt("sync", sTE, lnmm3[icb - 2])
                lnl = dml("sync", lambda h, icb=icb, par=par: h.dma_start(
                    out=lnwf[par][:, :],
                    in_=lnw[0:1, icb * 512:(icb + 1) * 512]), sLD)
                wt("tensor", sLD, lnl)
                if icb >= 1:
                    wt("tensor", sAC, wbc3_ev)
                em("tensor", lambda h, par=par: h.matmul(
                    ps4[:, :], ones_rf[:, :], lnwf[par][:, :],
                    start=True, stop=True), sTE, 1)
                t_wb3 = sTE.n
                lnmm3[icb] = t_wb3
                wt("scalar", sTE, t_wb3)
                if icb >= 2:
                    wt("scalar", sVE, wbc3_rd.get(icb - 2, 0))
                em("scalar", lambda h, par=par: h.activation(
                    wbcv[par][:, :], ps4[:, :], AF.Identity, scale=1.0), sAC, 1)
                wbc3_ev = sAC.n
                for ii in iis:
                    wt("vector", sAC, wbc3_ev)
                    wt("vector", sTE, rnd_prev)
                    em("vector", lambda h, ii=ii, icb=icb, par=par, pr=pr: h.scalar_tensor_tensor(
                        scf[0][:, :],
                        hslab[:, ii * FI + icb * 512:ii * FI + icb * 512 + 512],
                        sA3[:, ii:ii + 1], wbcv[par][:, :],
                        op0=ALU.mult, op1=ALU.mult), sVE, 1)
                    em("vector", lambda h: h.tensor_scalar(
                        scf[1][:, :], scf[0][:, :], MAGIC, -MAGIC,
                        op0=ALU.add, op1=ALU.add), sVE, 1)
                    v_rq = sVE.n
                    if icb < 2 and ii == iis[0]:
                        wbc3_rd = wbc3_rd if (icb or pr or half) else {}
                    wbc3_rd[icb] = sVE.n
                    wt("tensor", sVE, v_rq)
                    wt("tensor", sAC, prevQ["ev"])
                    for j in range(4):
                        em("tensor", lambda h, j=j: h.transpose(
                            ps5[:, j * 128:(j + 1) * 128],
                            scf[1][:, j * 128:(j + 1) * 128], id_f[:, :]), sTE, 1)
                    t_tq = sTE.n
                    rnd_prev = t_tq
                    wt("scalar", sTE, t_tq)
                    em("scalar", lambda h, ii=ii, icb=icb, pr=pr: h.activation(
                        qhT[:, (ii - 2 * pr) * FI + icb * 512:
                            (ii - 2 * pr) * FI + icb * 512 + 512],
                        ps5[:, :], AF.Identity, scale=1.0), sAC, 1)
                    prevQ["ev"] = sAC.n
            a_qhT = sAC.n
            if debug and half == 0 and pr == 0:
                wt("gpsimd", sAC, a_qhT)
                em("gpsimd", lambda h: h.dma_start(
                    out=dbg["qhT"][:, :], in_=qhT[:, :]), sST, 16)
            wt("tensor", sAC, a_qhT)
            yev_prev = 0
            for db in range(4):
                for sb_ in range(8):
                    qe = "sync"
                    qc = sLD
                    half_off = (sb_ % 2) * 4096
                    if db or sb_ >= 2:
                        wt(qe, sTE, wd_mm.get((db, sb_ - 2) if sb_ >= 2 else (db - 1, sb_ + 6), 0))
                    wl = dml(qe, lambda h, sb_=sb_, db=db, off=half_off: h.dma_start(
                        out=wsl[:, off:off + 4096].rearrange("p (c i) -> p c i", c=8),
                        in_=agd_out[sb_ * 1024:(sb_ + 1) * 1024,
                                    db * 512:(db + 1) * 512].rearrange(
                            "(c p) i -> p c i", p=128)), qc)
                    wt("tensor", qc, wl)
                    if db == 0 and sb_ == 0:
                        wt("tensor", sVE, yev_prev)
                    if sb_ == 0:
                        wt("tensor", sVE, yev_prev)
                    for c8 in range(8):
                        ic = sb_ * 8 + c8
                        for bi, ii in enumerate(iis):
                            em("tensor", lambda h, ic=ic, c8=c8, bi=bi, ii=ii,
                               pr=pr, off=half_off: h.matmul(
                                banksD[bi],
                                qhT[:, (ii - 2 * pr) * FI + ic * 128:
                                    (ii - 2 * pr) * FI + ic * 128 + 128],
                                wsl[:, off + c8 * 512:off + (c8 + 1) * 512],
                                start=(ic == 0), stop=(ic == 63)), sTE, 1)
                    if db < 1 and sb_ < 2:
                        wd_mm = wd_mm if (db or sb_ or pr or half) else {}
                    wd_mm[(db, sb_)] = sTE.n
                t_dn = sTE.n
                qe = "sync"
                qc = sLD
                xb_ld = dml(qe, lambda h, pr=pr, db=db, half=half: h.dma_start(
                    out=x2b[db % 2][:, :].rearrange("p (i d) -> p i d", i=2),
                    in_=x2d[(half * 4 + 2 * pr) * 128:(half * 4 + 2 * pr + 2) * 128,
                            db * 512:(db + 1) * 512].rearrange(
                        "(i p) d -> p i d", p=128)), qc)
                for bi, ii in enumerate(iis):
                    tti = tts[ii]
                    wt("vector", sTE, t_dn)
                    wt("vector", qc, xb_ld)
                    if yli >= 2:
                        wt("vector", sST, y_st.get(yli - 2, 0))
                    em("vector", lambda h, bi=bi, tti=tti, yli=yli, db=db: h.scalar_tensor_tensor(
                        yb[yli % 2][:, :], banksD[bi], c3s[:, tti:tti + 1],
                        x2b[db % 2][:, bi * 512:(bi + 1) * 512],
                        op0=ALU.mult, op1=ALU.add), sVE, 1)
                    yev_prev = sVE.n
                    wt("gpsimd", sVE, sVE.n)
                    y_st[yli] = dml("gpsimd", lambda h, tti=tti, db=db, yli=yli: h.dma_start(
                        out=y[tti * 128:(tti + 1) * 128, db * 512:(db + 1) * 512],
                        in_=yb[yli % 2][:, :]), sST)
                    yli += 1

    A_BF.release(); A_F.release()
    A_BF.release(); A_F.release()

    wt("gpsimd", sST, sST.n)
    _finish(nc, plan)
    es.close()
    return nc


def _finish(nc, plan):
    with nc.Block() as block:
        @block.sync
        def _(h):
            for f in plan["sync"]:
                f(h)

        @block.tensor
        def _(h):
            for f in plan["tensor"]:
                f(h)

        @block.vector
        def _(h):
            for f in plan["vector"]:
                f(h)

        @block.scalar
        def _(h):
            for f in plan["scalar"]:
                f(h)

        @block.gpsimd
        def _(h):
            for f in plan["gpsimd"]:
                f(h)


_BUILD_CACHE = {}


def _get_nc(debug=False, phases="ABCD"):
    key = (debug, phases)
    if key not in _BUILD_CACHE:
        _BUILD_CACHE[key] = build_nc(debug=debug, phases=phases)
    return _BUILD_CACHE[key]


def make_in_maps(L0_attn_x, L0_attn_W_Q, L0_attn_W_K, L0_attn_W_V, L0_attn_W_O,
                 L0_attn_b_Q, L0_attn_b_K, L0_attn_b_V, L0_attn_b_O,
                 L0_ffn_w_gate, L0_ffn_w_up, L0_ffn_w_down, L0_ffn_ln_weight):
    f = np.ascontiguousarray
    x = np.asarray(L0_attn_x, np.float32).reshape(S, D)
    xTh = f(x.T)
    wgT = f(np.asarray(L0_ffn_w_gate, np.float32).T)
    wuT = f(np.asarray(L0_ffn_w_up, np.float32).T)
    wdT = f(np.asarray(L0_ffn_w_down, np.float32).T)
    woh = f(np.asarray(L0_attn_W_O, np.float32).reshape(D, D))
    idf = np.eye(128, dtype=np.float32)
    WQ = np.asarray(L0_attn_W_Q, np.float32)
    WK = np.asarray(L0_attn_W_K, np.float32)
    WV = np.asarray(L0_attn_W_V, np.float32)
    bQ = np.asarray(L0_attn_b_Q, np.float32)
    bK = np.asarray(L0_attn_b_K, np.float32)
    bV = np.asarray(L0_attn_b_V, np.float32)
    in_maps = []
    for c in range(NC):
        h0 = HPC * c
        in_maps.append({
            "xT": xTh,
            "xr": f(x[c * TPC:(c + 1) * TPC]),
            "wq": f(WQ[h0:h0 + HPC]),
            "wk": f(WK[h0:h0 + HPC]),
            "wv": f(WV[h0:h0 + HPC]),
            "bqkv": f(np.concatenate([bQ[h0:h0 + HPC], bK[h0:h0 + HPC],
                                      bV[h0:h0 + HPC]], axis=0)),
            "wo": woh,
            "bo": np.asarray(L0_attn_b_O, np.float32).reshape(1, D),
            "wg_sl": f(wgT[c * (D // NC):(c + 1) * (D // NC)]),
            "wu_sl": f(wuT[c * (D // NC):(c + 1) * (D // NC)]),
            "wd_sl": f(wdT[c * (FI // NC):(c + 1) * (FI // NC)]),
            "lnw": np.asarray(L0_ffn_ln_weight, np.float32).reshape(1, FI),
            "idf": idf,
        })
    return in_maps


def kernel(**inputs):
    nc = _get_nc()
    in_maps = make_in_maps(**inputs)
    res = run_bass_kernel_spmd(nc, in_maps, core_ids=list(range(NC)))
    out = np.concatenate([np.asarray(r["y"], np.float32) for r in res.results],
                         axis=0)
    return out.reshape(1, S, D)
